# version 02
# baseline (speedup 1.0000x reference)
import numpy as np
import ml_dtypes
from contextlib import ExitStack

import concourse.bass as bass
import concourse.mybir as mybir
from concourse.bass import IndirectOffsetOnAxis
from concourse.bass_utils import run_bass_kernel_spmd

F32 = mybir.dt.float32
BF16 = mybir.dt.bfloat16
I32 = mybir.dt.int32
AF = mybir.ActivationFunctionType
ALU = mybir.AluOpType
AX = mybir.AxisListType

NCORES = 8
TPC = 1024
DM = 2048
NDC = DM // 128
SEQ = 4096

COMPUTE_ENGINES = ("tensor", "vector", "scalar", "gpsimd")
ENGINES = COMPUTE_ENGINES + ("sync",)


class Sem:
    def __init__(self, prog, name):
        self.h = prog.ctx.enter_context(prog.nc.semaphore(name))
        self.n = 0


class Prog:

    def __init__(self, nc, ctx):
        self.nc = nc
        self.ctx = ctx
        self.qs = {k: [] for k in ENGINES}
        self.e = {k: Sem(self, f"E_{k}") for k in COMPUTE_ENGINES}

    def sem(self, name):
        return Sem(self, name)

    def sbuf(self, name, shape, dtype=F32):
        return self.ctx.enter_context(self.nc.sbuf_tensor(name, shape, dtype))

    def psum(self, name, shape, dtype=F32):
        return self.ctx.enter_context(self.nc.psum_tensor(name, shape, dtype))

    def op(self, eng, fn, chain=True):
        s = self.e[eng]
        if chain and s.n > 0:
            self.wait(eng, s, s.n)
        s.n += 1
        h, v = s.h, s.n
        self.qs[eng].append(lambda e: fn(e).then_inc(h, 1))
        return s.n

    def T(self, fn, chain=True):
        return self.op("tensor", fn, chain)

    def V(self, fn, chain=True):
        return self.op("vector", fn, chain)

    def S(self, fn, chain=True):
        return self.op("scalar", fn, chain)

    def G(self, fn, chain=True):
        return self.op("gpsimd", fn, chain)

    def wait(self, eng, sem, n=None):
        v = sem.n if n is None else n
        if v <= 0:
            return
        h = sem.h
        self.qs[eng].append(lambda e: e.wait_ge(h, v))

    def dma(self, q, out, in_, sem):
        sem.n += 16
        h = sem.h
        self.qs[q].append(lambda e: e.dma_start(out=out, in_=in_).then_inc(h, 16))
        return sem.n

    def idma(self, out, in_, idx_ap, axis, sem):
        sem.n += 16
        h = sem.h
        self.qs["gpsimd"].append(lambda e: e.indirect_dma_start(
            out=out, out_offset=None, in_=in_,
            in_offset=IndirectOffsetOnAxis(idx_ap, axis)).then_inc(h, 16))
        return sem.n

    def finish(self):
        block = self.ctx.enter_context(self.nc.Block())
        for eng in ENGINES:
            fns = self.qs[eng]
            if fns:
                getattr(block, eng)(lambda e, fns=fns: [f(e) for f in fns])


class Ring:

    def __init__(self, p, name, R):
        self.sems = [Sem(p, f"{name}{r}") for r in range(R)]
        self.R = R

    def __getitem__(self, i):
        return self.sems[i % self.R]


def fm_dram_ap(dram_ap, nchunks):
    return dram_ap.rearrange("(c p) t -> p c t", p=128)


def fm_load(p, sb, dram_ap, nchunks, sem, q="sync"):
    return p.dma(q, sb[:].rearrange("p (c t) -> p c t", c=nchunks), fm_dram_ap(dram_ap, nchunks), sem)


def build_ln(nadd=0, emit_sum=False):
    nc = bass.Bass("TRN2", target_bir_lowering=False)
    x_d = nc.dram_tensor("x", [TPC, DM], F32, kind="ExternalInput").ap()
    adds = [nc.dram_tensor(f"add{j}", [TPC, DM], F32, kind="ExternalInput").ap() for j in range(nadd)]
    xn_d = nc.dram_tensor("xn", [TPC, DM], F32, kind="ExternalOutput").ap()
    xs_d = nc.dram_tensor("xsum", [TPC, DM], F32, kind="ExternalOutput").ap() if emit_sum else None
    ntiles = TPC // 128
    ctx = ExitStack()
    with ctx:
        p = Prog(nc, ctx)
        eV, eS = p.e["vector"], p.e["scalar"]
        xt = [p.sbuf(f"xt{i}", [128, DM]) for i in range(2)]
        at = [p.sbuf(f"at{i}", [128, DM]) for i in range(2)]
        ot = [p.sbuf(f"ot{i}", [128, DM]) for i in range(2)]
        junk = p.sbuf("junk", [128, DM])
        s1 = p.sbuf("s1", [128, ntiles])
        s2 = p.sbuf("s2", [128, ntiles])
        mean = p.sbuf("mean", [128, ntiles])
        var = p.sbuf("var", [128, ntiles])
        rstd = p.sbuf("rstd", [128, ntiles])
        cinv = p.sbuf("cinv", [128, 1])
        ldr = Ring(p, "ld", 2)
        str_ = Ring(p, "st", 2)

        cv = p.V(lambda e: e.memset(cinv[:], 1.0 / DM))
        store_done = []

        for i in range(ntiles):
            b = i % 2
            xs = xt[b]
            lds = ldr[i]
            sts = str_[i]
            rows = slice(i * 128, (i + 1) * 128)
            if i >= 2:
                p.wait("sync", *store_done[i - 2])
            ldn = p.dma("sync", xs[:], x_d[rows, :], lds)
            for aj, ad in enumerate(adds):
                abuf = at[b]
                if aj > 0:
                    p.wait("sync", eV, eV.n)
                ldn = p.dma("sync", abuf[:], ad[rows, :], lds)
                p.wait("vector", lds, ldn)
                p.V(lambda e, xs=xs, abuf=abuf: e.tensor_tensor(xs[:], xs[:], abuf[:], ALU.add))
            if adds:
                xready = eV.n
                p.wait("scalar", eV, xready)
                if emit_sum:
                    p.wait("sync", eV, xready)
                    p.dma("sync", xs_d[rows, :], xs[:], sts)
            else:
                p.wait("vector", lds, ldn)
                p.wait("scalar", lds, ldn)
            si = slice(i, i + 1)
            v_s1 = p.V(lambda e, xs=xs, si=si: e.reduce_sum(s1[:, si], xs[:], axis=AX.X))
            a_s2 = p.S(lambda e, xs=xs, si=si: e.activation(junk[:], xs[:], AF.Square, accum_out=s2[:, si]))
            p.wait("vector", eS, a_s2)
            p.V(lambda e, si=si: e.tensor_scalar(mean[:, si], s1[:, si], cinv[:], None, ALU.mult))
            p.V(lambda e, si=si: e.tensor_tensor(var[:, si], mean[:, si], mean[:, si], ALU.mult))
            p.V(lambda e, si=si: e.scalar_tensor_tensor(var[:, si], s2[:, si], cinv[:], var[:, si], ALU.mult, ALU.subtract))
            v_var = p.V(lambda e, si=si: e.tensor_single_scalar(var[:, si], var[:, si], 1e-5, ALU.add))
            p.wait("scalar", eV, v_var)
            a_sqrt = p.S(lambda e, si=si: e.activation(var[:, si], var[:, si], AF.Sqrt))
            p.wait("vector", eS, a_sqrt)
            p.V(lambda e, si=si: e.reciprocal(rstd[:, si], var[:, si]))
            if i >= 2:
                p.wait("vector", *store_done[i - 2])
            ob = ot[b]
            v_norm = p.V(lambda e, xs=xs, ob=ob, i=i: e.tensor_scalar(
                ob[:], xs[:], mean[:, i:i + 1], rstd[:, i:i + 1], ALU.subtract, ALU.mult))
            p.wait("sync", eV, v_norm)
            p.dma("sync", xn_d[rows, :], ob[:], sts)
            store_done.append((sts, sts.n))
        for s in str_.sems:
            p.wait("sync", s, s.n)
        p.finish()
    return nc


def build_dbc_delta():
    nc = bass.Bass("TRN2", target_bir_lowering=False)
    xnT_d = nc.dram_tensor("xnT", [DM, TPC], F32, kind="ExternalInput").ap()
    winT_d = nc.dram_tensor("W_inT", [DM, 96], F32, kind="ExternalInput").ap()
    wdtT_d = nc.dram_tensor("W_dtT", [64, DM], F32, kind="ExternalInput").ap()
    bdt_d = nc.dram_tensor("b_dt_r", [128, NDC], F32, kind="ExternalInput").ap()
    dbcT_d = nc.dram_tensor("dbcT", [96, TPC], F32, kind="ExternalOutput").ap()
    deltaT_d = nc.dram_tensor("deltaT", [DM, TPC], F32, kind="ExternalOutput").ap()
    ctx = ExitStack()
    with ctx:
        p = Prog(nc, ctx)
        eT, eV, eS = p.e["tensor"], p.e["vector"], p.e["scalar"]
        xn = p.sbuf("xn", [128, NDC * TPC])
        win = p.sbuf("win", [128, NDC * 96])
        wdt = p.sbuf("wdt", [64, DM])
        bdt = p.sbuf("bdt", [128, NDC])
        dbc = p.sbuf("dbc", [96, TPC])
        v_sb = [p.sbuf(f"v_sb{i}", [128, 512]) for i in range(2)]
        r_sb = [p.sbuf(f"r_sb{i}", [128, 512]) for i in range(2)]
        w_sb = [p.sbuf(f"w_sb{i}", [128, 512]) for i in range(2)]
        el_sb = [p.sbuf(f"el_sb{i}", [128, 512]) for i in range(2)]
        dl_sb = [p.sbuf(f"dl_sb{i}", [128, 512]) for i in range(2)]
        one = p.sbuf("one", [128, 1])
        ps1 = p.psum("ps1", [96, 512])
        ps2 = p.psum("ps2", [96, 512])
        psd = [p.psum(f"psd{i}", [128, 512]) for i in range(2)]
        ld = p.sem("ld")
        stb = p.sem("stb")
        str_ = Ring(p, "st", 2)

        p.V(lambda e: e.memset(one[:], 1.0))

        fm_load(p, xn, xnT_d, NDC, ld)
        fm_load(p, win, winT_d, NDC, ld)
        p.dma("sync", wdt[:], wdtT_d[:], ld)
        p.dma("sync", bdt[:], bdt_d[:], ld)
        ldall = ld.n
        p.wait("tensor", ld, ldall)
        p.wait("vector", ld, ldall)
        p.wait("scalar", ld, ldall)

        mm_done = []
        for th in range(2):
            ps = ps1 if th == 0 else ps2
            for c in range(NDC):
                n = p.T(lambda e, ps=ps, c=c, th=th: e.matmul(
                    ps[:, :], win[:, c * 96:(c + 1) * 96],
                    xn[:, c * TPC + th * 512: c * TPC + th * 512 + 512],
                    start=(c == 0), stop=(c == NDC - 1)))
            mm_done.append(n)
        p.wait("vector", eT, mm_done[0])
        p.V(lambda e: e.tensor_copy(dbc[:, 0:512], ps1[:, :]))
        p.wait("vector", eT, mm_done[1])
        dbc_v = p.V(lambda e: e.tensor_copy(dbc[:, 512:1024], ps2[:, :]))
        p.wait("sync", eV, dbc_v)
        p.dma("sync", dbcT_d[:], dbc[:], stb)
        p.wait("tensor", eV, dbc_v)

        store_done = []
        psum_free = {}
        for c in range(NDC):
            for th in range(2):
                k = c * 2 + th
                par = k % 2
                ps = psd[par]
                if k >= 2:
                    p.wait("tensor", eV, psum_free[k - 2])
                mmn = p.T(lambda e, ps=ps, c=c, th=th: e.matmul(
                    ps[:, :], wdt[:, c * 128:(c + 1) * 128],
                    dbc[0:64, th * 512:(th + 1) * 512], start=True, stop=True))
                p.wait("vector", eT, mmn)
                p.V(lambda e, ps=ps, c=c, par=par: e.tensor_scalar(
                    v_sb[par][:], ps[:, :], bdt[:, c:c + 1], None, ALU.add))
                psum_free[k] = p.V(lambda e, par=par: e.tensor_single_scalar(
                    r_sb[par][:], v_sb[par][:], 0.0, ALU.max))
                wv = p.V(lambda e, par=par: e.scalar_tensor_tensor(
                    w_sb[par][:], r_sb[par][:], -2.0, v_sb[par][:], ALU.mult, ALU.add))
                p.wait("scalar", eV, wv)
                p.S(lambda e, par=par: e.activation(el_sb[par][:], w_sb[par][:], AF.Exp))
                lnn = p.S(lambda e, par=par: e.activation(el_sb[par][:], el_sb[par][:], AF.Ln, bias=one[:]))
                if k >= 2:
                    p.wait("vector", *store_done[k - 2])
                p.wait("vector", eS, lnn)
                dl = p.V(lambda e, par=par: e.tensor_tensor(dl_sb[par][:], r_sb[par][:], el_sb[par][:], ALU.add))
                p.wait("sync", eV, dl)
                sts = str_[k]
                p.dma("sync", deltaT_d[c * 128:(c + 1) * 128, th * 512:(th + 1) * 512], dl_sb[par][:], sts)
                store_done.append((sts, sts.n))
        p.wait("sync", stb, stb.n)
        for s in str_.sems:
            p.wait("sync", s, s.n)
        p.finish()
    return nc


def build_scan():
    TT = NCORES * TPC
    nc = bass.Bass("TRN2", target_bir_lowering=False)
    deltaT_d = nc.dram_tensor("deltaT_sl", [256, TT], F32, kind="ExternalInput").ap()
    xnT_d = nc.dram_tensor("xnT_sl", [256, TT], F32, kind="ExternalInput").ap()
    bt_d = nc.dram_tensor("Bt", [16, TT], F32, kind="ExternalInput").ap()
    ct_d = nc.dram_tensor("Ct", [16, TT], F32, kind="ExternalInput").ap()
    alog_d = nc.dram_tensor("A_log_r", [128, 32], F32, kind="ExternalInput").ap()
    dr_d = nc.dram_tensor("D_r", [128, 2], F32, kind="ExternalInput").ap()
    y_d = nc.dram_tensor("yT_sl", [256, TT], F32, kind="ExternalOutput").ap()
    ctx = ExitStack()
    with ctx:
        p = Prog(nc, ctx)
        eT, eV, eS = p.e["tensor"], p.e["vector"], p.e["scalar"]
        delta = [p.sbuf("delta0", [128, SEQ])]
        dxq = [p.sbuf("dxq0", [128, SEQ])]
        yq = [p.sbuf("yq0", [128, SEQ])]
        bt = p.sbuf("bt", [16, SEQ])
        ct = p.sbuf("ct", [16, SEQ])
        acol = p.sbuf("acol", [128, 32])
        dcol = p.sbuf("dcol", [128, 2])
        ones1 = p.sbuf("ones1", [1, 128])
        esel = p.sbuf("esel", [16, DM])
        tsel = p.sbuf("tsel", [16, DM])
        bbc = p.sbuf("bbc", [128, SEQ])
        cbc = p.sbuf("cbc", [128, SEQ])
        a_t = p.sbuf("a_t", [128, SEQ])
        b_t = p.sbuf("b_t", [128, SEQ])
        h_t = p.sbuf("h_t", [128, SEQ])
        tmp = p.sbuf("tmp", [128, SEQ])
        psb = p.psum("psb", [128, 512])
        psc = p.psum("psc", [128, 512])
        ld0 = p.sem("ld0")
        ldq = Ring(p, "ldq", 2)
        str_ = Ring(p, "st", 2)

        p.G(lambda e: e.memset(ones1[:], 1.0))
        p.G(lambda e: e.iota(esel[:].rearrange("k (n q) -> k n q", q=128),
                             pattern=[[1, 16], [0, 128]], base=0, channel_multiplier=0,
                             allow_small_or_imprecise_dtypes=True))
        g_io = p.G(lambda e: e.iota(tsel[:], pattern=[[0, DM]], base=0, channel_multiplier=1,
                                    allow_small_or_imprecise_dtypes=True))
        p.wait("vector", p.e["gpsimd"], g_io)
        esel_v = p.V(lambda e: e.tensor_tensor(esel[:], esel[:], tsel[:], ALU.is_equal))
        p.wait("tensor", eV, esel_v)
        p.dma("sync", acol[:], alog_d[:], ld0)
        p.dma("sync", dcol[:], dr_d[:], ld0)
        ld0n = ld0.n

        p.wait("scalar", ld0, ld0n)
        a_exp = p.S(lambda e: e.activation(acol[:], acol[:], AF.Exp))
        p.wait("vector", eS, a_exp)
        acol_v = p.V(lambda e: e.tensor_single_scalar(acol[:], acol[:], -1.0, ALU.mult))
        p.wait("tensor", ld0, ld0n)
        p.wait("tensor", p.e["gpsimd"], 1)
        p.wait("scalar", eV, acol_v)
        p.wait("vector", ld0, ld0n)

        store_done = []
        for qi, (Dc, b) in enumerate([(d, b) for d in range(2) for b in range(2)]):
            par = 0
            lds = ldq[qi]
            rows = slice(Dc * 128, (Dc + 1) * 128)
            cols = slice(b * SEQ, (b + 1) * SEQ)
            if qi >= 1:
                p.wait("sync", *store_done[qi - 1])
            p.dma("sync", delta[par][:], deltaT_d[rows, cols], lds)
            p.dma("sync", bt[:], bt_d[:, cols], lds)
            p.dma("sync", ct[:], ct_d[:, cols], lds)
            ldn = p.dma("sync", dxq[par][:], xnT_d[rows, cols], lds)
            p.wait("vector", lds, ldn)
            p.wait("tensor", lds, ldn)
            p.V(lambda e, par=par, Dc=Dc: e.tensor_scalar(
                yq[par][:], dxq[par][:], dcol[:, Dc:Dc + 1], None, ALU.mult))
            dx_v = p.V(lambda e, par=par: e.tensor_tensor(
                dxq[par][:], delta[par][:], dxq[par][:], ALU.mult))
            p.wait("scalar", lds, ldn)
            for n in range(16):
                for j in range(8):
                    sl = slice(j * 512, (j + 1) * 512)
                    dsl = slice(j * 512, (j + 1) * 512)
                    m1 = p.T(lambda e, sl=sl, n=n: e.matmul(
                        psb[:, :], esel[:, n * 128:(n + 1) * 128], bt[:, sl], start=True, stop=True))
                    m2 = p.T(lambda e, sl=sl, n=n: e.matmul(
                        psc[:, :], esel[:, n * 128:(n + 1) * 128], ct[:, sl], start=True, stop=True))
                    p.wait("vector", eT, m2)
                    p.V(lambda e, dsl=dsl: e.tensor_copy(bbc[:, dsl], psb[:, :]))
                    vj = p.V(lambda e, dsl=dsl: e.tensor_copy(cbc[:, dsl], psc[:, :]))
                    p.wait("tensor", eV, vj)
                col = Dc * 16 + n
                p.wait("scalar", eV, eV.n)
                an = p.S(lambda e, par=par, col=col: e.activation(
                    a_t[:], delta[par][:], AF.Exp, scale=acol[:, col:col + 1]))
                p.V(lambda e, par=par: e.tensor_tensor(b_t[:], dxq[par][:], bbc[:], ALU.mult))
                p.wait("vector", eS, an)
                p.V(lambda e: e.tensor_tensor_scan(h_t[:], a_t[:], b_t[:], 0.0, ALU.mult, ALU.add))
                p.V(lambda e: e.tensor_tensor(tmp[:], h_t[:], cbc[:], ALU.mult))
                p.V(lambda e, par=par: e.tensor_tensor(yq[par][:], yq[par][:], tmp[:], ALU.add))
            p.wait("sync", eV, eV.n)
            sts = str_[qi]
            p.dma("sync", y_d[rows, cols], yq[par][:], sts)
            store_done.append((sts, sts.n))
        for s in str_.sems:
            p.wait("sync", s, s.n)
        p.finish()
    return nc





def build_mlp():
    NIC = 64
    nc = bass.Bass("TRN2", target_bir_lowering=False)
    xnT_d = nc.dram_tensor("xnT", [DM, TPC], F32, kind="ExternalInput").ap()
    gate_d = nc.dram_tensor("gate_r", [NIC, NDC, 128, 128], F32, kind="ExternalInput").ap()
    up_d = nc.dram_tensor("up_r", [NIC, NDC, 128, 128], F32, kind="ExternalInput").ap()
    down_d = nc.dram_tensor("down_r", [NDC, NIC, 128, 128], F32, kind="ExternalInput").ap()
    mlpT_d = nc.dram_tensor("mlpT", [DM, TPC], F32, kind="ExternalOutput").ap()
    ctx = ExitStack()
    with ctx:
        p = Prog(nc, ctx)
        eT, eV, eS = p.e["tensor"], p.e["vector"], p.e["scalar"]
        xn_bf = p.sbuf("xn_bf", [128, NDC * TPC], BF16)
        gsl = [p.sbuf(f"gsl{i}", [128, NDC * 128], BF16) for i in range(2)]
        usl = [p.sbuf(f"usl{i}", [128, NDC * 128], BF16) for i in range(2)]
        dsl = [p.sbuf(f"dsl{i}", [128, NIC * 128], BF16) for i in range(2)]
        act = p.sbuf("act", [128, NIC * 512], BF16)
        sg = p.sbuf("sg", [128, 512])
        mo = [p.sbuf(f"mo{i}", [128, 512]) for i in range(2)]
        psg = p.psum("psg", [128, 512])
        psu = p.psum("psu", [128, 512])
        psd = [p.psum(f"psd{i}", [128, 512]) for i in range(2)]
        ldx = p.sem("ldx")
        ldg = Ring(p, "ldg", 2)
        ldd = Ring(p, "ldd", 2)
        str_ = Ring(p, "st", 2)

        p.dma("gpsimd", xn_bf[:].rearrange("p (c t) -> p c t", c=NDC), fm_dram_ap(xnT_d, NDC), ldx)
        p.wait("tensor", ldx, ldx.n)

        gu_done = {}
        down_mm = {}
        store_done = []
        n_store = 0
        for th in range(2):
            tsl = slice(th * 512, (th + 1) * 512)
            for ic in range(NIC):
                gi = th * NIC + ic
                par = gi % 2
                lds = ldg[gi]
                if gi >= 2:
                    p.wait("gpsimd", eV, gu_done[gi - 2])
                p.dma("gpsimd", gsl[par][:].rearrange("p (c i) -> p c i", c=NDC),
                      gate_d[ic].rearrange("c p i -> p c i"), lds)
                ldn = p.dma("gpsimd", usl[par][:].rearrange("p (c i) -> p c i", c=NDC),
                            up_d[ic].rearrange("c p i -> p c i"), lds)
                p.wait("tensor", lds, ldn)
                for c in range(NDC):
                    p.T(lambda e, par=par, c=c, tsl=tsl: e.matmul(
                        psg[:, :], gsl[par][:, c * 128:(c + 1) * 128],
                        xn_bf[:, c * TPC + tsl.start: c * TPC + tsl.stop], start=(c == 0), stop=(c == NDC - 1)))
                mmu = None
                for c in range(NDC):
                    mmu = p.T(lambda e, par=par, c=c, tsl=tsl: e.matmul(
                        psu[:, :], usl[par][:, c * 128:(c + 1) * 128],
                        xn_bf[:, c * TPC + tsl.start: c * TPC + tsl.stop], start=(c == 0), stop=(c == NDC - 1)))
                p.wait("scalar", eT, mmu)
                sgn = p.S(lambda e: e.activation(sg[:], psg[:, :], AF.Silu))
                p.wait("vector", eS, sgn)
                gu_done[gi] = p.V(lambda e, ic=ic: e.tensor_tensor(
                    act[:, ic * 512:(ic + 1) * 512], sg[:], psu[:, :], ALU.mult))
                p.wait("tensor", eV, gu_done[gi])
            for dc in range(NDC):
                gj = th * NDC + dc
                par = gj % 2
                lds = ldd[gj]
                if gj >= 2:
                    p.wait("gpsimd", eT, down_mm[gj - 2])
                ldn = p.dma("gpsimd", dsl[par][:].rearrange("p (c d) -> p c d", c=NIC),
                            down_d[dc].rearrange("c p d -> p c d"), lds)
                p.wait("tensor", lds, ldn)
                mmn = None
                for ic in range(NIC):
                    mmn = p.T(lambda e, par=par, ic=ic, gj=gj: e.matmul(
                        psd[gj % 2][:, :], dsl[par][:, ic * 128:(ic + 1) * 128],
                        act[:, ic * 512:(ic + 1) * 512], start=(ic == 0), stop=(ic == NIC - 1)))
                down_mm[gj] = mmn
                p.wait("vector", eT, mmn)
                if n_store >= 2:
                    p.wait("vector", *store_done[n_store - 2])
                mon = p.V(lambda e, gj=gj: e.tensor_copy(mo[gj % 2][:], psd[gj % 2][:, :]))
                p.wait("tensor", eV, mon)
                p.wait("sync", eV, mon)
                sts = str_[gj]
                p.dma("sync", mlpT_d[dc * 128:(dc + 1) * 128, tsl], mo[gj % 2][:], sts)
                store_done.append((sts, sts.n))
                n_store += 1
        for s in str_.sems:
            p.wait("sync", s, s.n)
        p.finish()
    return nc


def build_qproj():
    nc = bass.Bass("TRN2", target_bir_lowering=False)
    xnT_d = nc.dram_tensor("xnT", [DM, TPC], F32, kind="ExternalInput").ap()
    wq_d = nc.dram_tensor("W_q", [DM, 128], F32, kind="ExternalInput").ap()
    qT_d = nc.dram_tensor("qT", [128, TPC], F32, kind="ExternalOutput").ap()
    ctx = ExitStack()
    with ctx:
        p = Prog(nc, ctx)
        eT, eV = p.e["tensor"], p.e["vector"]
        xn = p.sbuf("xn", [128, NDC * TPC])
        wq = p.sbuf("wq", [128, NDC * 128])
        qo = p.sbuf("qo", [128, TPC])
        ps = [p.psum(f"ps{i}", [128, 512]) for i in range(2)]
        ld = p.sem("ld")
        st = p.sem("st")
        fm_load(p, xn, xnT_d, NDC, ld)
        fm_load(p, wq, wq_d, NDC, ld)
        p.wait("tensor", ld, ld.n)
        for th in range(2):
            mmn = None
            for c in range(NDC):
                mmn = p.T(lambda e, th=th, c=c: e.matmul(
                    ps[th][:, :], wq[:, c * 128:(c + 1) * 128],
                    xn[:, c * TPC + th * 512: c * TPC + th * 512 + 512],
                    start=(c == 0), stop=(c == NDC - 1)))
            p.wait("vector", eT, mmn)
            p.V(lambda e, th=th: e.tensor_copy(qo[:, th * 512:(th + 1) * 512], ps[th][:, :]))
        p.wait("sync", eV, eV.n)
        p.dma("sync", qT_d[:], qo[:], st)
        p.wait("sync", st, st.n)
        p.finish()
    return nc


def build_peer():
    NE = 4096
    nc = bass.Bass("TRN2", target_bir_lowering=False)
    xntok_d = nc.dram_tensor("xn_tok", [TPC, DM], F32, kind="ExternalInput").ap()
    q0_d = nc.dram_tensor("Q0T", [64, TPC], F32, kind="ExternalInput").ap()
    q1_d = nc.dram_tensor("Q1T", [64, TPC], F32, kind="ExternalInput").ap()
    keys_d = nc.dram_tensor("keys_r", [64, 128], F32, kind="ExternalInput").ap()
    de_d = nc.dram_tensor("down_embed", [NE, DM], F32, kind="ExternalInput").ap()
    ue_d = nc.dram_tensor("up_embed", [NE, DM], F32, kind="ExternalInput").ap()
    y_d = nc.dram_tensor("peer_tok", [TPC, DM], F32, kind="ExternalOutput").ap()
    ntiles = TPC // 128
    ctx = ExitStack()
    with ctx:
        p = Prog(nc, ctx)
        eT, eV, eS, eG = p.e["tensor"], p.e["vector"], p.e["scalar"], p.e["gpsimd"]
        q0 = p.sbuf("q0", [64, TPC])
        q1 = p.sbuf("q1", [64, TPC])
        keys = p.sbuf("keys", [64, 128])
        xt = [p.sbuf(f"xt{i}", [128, DM]) for i in range(2)]
        yt = [p.sbuf(f"yt{i}", [128, DM]) for i in range(2)]
        ga = p.sbuf("ga", [128, 8 * DM])
        r0 = p.sbuf("r0", [128, 64])
        r1 = p.sbuf("r1", [128, 64])
        iota64 = p.sbuf("iota64", [128, 64])
        junk64 = p.sbuf("junk64", [128, 64])
        junk2k = p.sbuf("junk2k", [128, DM])
        sx8 = p.sbuf("sx8", [128, 8])
        sy8 = p.sbuf("sy8", [128, 8])
        ix8 = p.sbuf("ix8", [128, 8])
        iy8 = p.sbuf("iy8", [128, 8])
        all_s = p.sbuf("all_s", [128, 64])
        all_i = p.sbuf("all_i", [128, 64])
        sc8 = p.sbuf("sc8", [128, 8])
        fidx = p.sbuf("fidx", [128, 8])
        idx32 = p.sbuf("idx32", [128, 8], I32)
        negmax = p.sbuf("negmax", [128, 1])
        pexp = p.sbuf("pexp", [128, 8])
        ssum = p.sbuf("ssum", [128, 1])
        rsum = p.sbuf("rsum", [128, 1])
        ew8 = p.sbuf("ew8", [128, 8])
        sil8 = p.sbuf("sil8", [128, 8])
        w8 = p.sbuf("w8", [128, 8])
        psr = p.psum("psr", [128, 512])
        ld0 = p.sem("ld0")
        ldx = Ring(p, "ldx", 2)
        gdma = Ring(p, "gdma", 2)
        str_ = Ring(p, "st", 2)

        p.G(lambda e: e.iota(iota64[:], pattern=[[1, 64]], base=0, channel_multiplier=0,
                             allow_small_or_imprecise_dtypes=True))
        p.dma("sync", q0[:], q0_d[:], ld0)
        p.dma("sync", q1[:], q1_d[:], ld0)
        p.dma("sync", keys[:], keys_d[:], ld0)
        ld0n = ld0.n
        p.wait("tensor", ld0, ld0n)
        p.wait("vector", p.e["gpsimd"], 1)

        store_done = []
        for i in range(ntiles):
            par = i % 2
            lds = ldx[i]
            rows = slice(i * 128, (i + 1) * 128)
            msl = slice(i * 128, (i + 1) * 128)
            if i >= 2:
                p.wait("sync", *store_done[i - 2])
            ldn = p.dma("sync", xt[par][:], xntok_d[rows, :], lds)
            m0 = p.T(lambda e, msl=msl: e.matmul(psr[:, 0:64], q0[:, msl], keys[:, 0:64], start=True, stop=True))
            m1 = p.T(lambda e, msl=msl: e.matmul(psr[:, 64:128], q1[:, msl], keys[:, 64:128], start=True, stop=True))
            p.wait("vector", eT, m1)
            p.V(lambda e: e.tensor_copy(r0[:], psr[:, 0:64]))
            ev = p.V(lambda e: e.tensor_copy(r1[:], psr[:, 64:128]))
            p.wait("tensor", eV, ev)
            p.V(lambda e: e.max(out=sx8[:], in_=r0[:]))
            for c in range(8):
                p.V(lambda e, c=c: e.scalar_tensor_tensor(
                    junk64[:], r0[:], sx8[:, c:c + 1], iota64[:], ALU.is_equal, ALU.mult,
                    accum_out=ix8[:, c:c + 1]))
            p.V(lambda e: e.max(out=sy8[:], in_=r1[:]))
            for c in range(8):
                p.V(lambda e, c=c: e.scalar_tensor_tensor(
                    junk64[:], r1[:], sy8[:, c:c + 1], iota64[:], ALU.is_equal, ALU.mult,
                    accum_out=iy8[:, c:c + 1]))
            p.V(lambda e: e.tensor_tensor(
                all_s[:].rearrange("p (a c) -> p a c", c=8),
                sx8[:].rearrange("p (k o) -> p k o", o=1).to_broadcast([128, 8, 8]),
                sy8[:].rearrange("p (o k) -> p o k", o=1).to_broadcast([128, 8, 8]),
                ALU.add))
            p.V(lambda e: e.tensor_single_scalar(ix8[:], ix8[:], 64.0, ALU.mult))
            p.V(lambda e: e.tensor_tensor(
                all_i[:].rearrange("p (a c) -> p a c", c=8),
                ix8[:].rearrange("p (k o) -> p k o", o=1).to_broadcast([128, 8, 8]),
                iy8[:].rearrange("p (o k) -> p o k", o=1).to_broadcast([128, 8, 8]),
                ALU.add))
            p.V(lambda e: e.max(out=sc8[:], in_=all_s[:]))
            for c in range(8):
                p.V(lambda e, c=c: e.scalar_tensor_tensor(
                    junk64[:], all_s[:], sc8[:, c:c + 1], all_i[:], ALU.is_equal, ALU.mult,
                    accum_out=fidx[:, c:c + 1]))
            p.V(lambda e: e.tensor_single_scalar(fidx[:], fidx[:], 4095.0, ALU.min))
            p.V(lambda e: e.tensor_single_scalar(negmax[:], sc8[:, 0:1], -1.0, ALU.mult))
            vneg = eV.n
            p.wait("scalar", eV, vneg)
            aexp = p.S(lambda e: e.activation(pexp[:], sc8[:], AF.Exp, bias=negmax[:], accum_out=ssum[:]))
            p.wait("vector", eS, aexp)
            p.V(lambda e: e.reciprocal(rsum[:], ssum[:]))
            vidx = p.V(lambda e: e.tensor_copy(idx32[:], fidx[:]))
            p.wait("gpsimd", eV, vidx)
            gn = None
            for k in range(8):
                gn = p.idma(ga[:, k * DM:(k + 1) * DM], de_d[:, :],
                            idx32[:, k:k + 1], 0, gdma[i])
            p.wait("vector", lds, ldn)
            p.wait("vector", gdma[i], gn)
            for k in range(8):
                p.V(lambda e, k=k, par=par: e.scalar_tensor_tensor(
                    junk2k[:], ga[:, k * DM:(k + 1) * DM], 0.0, xt[par][:], ALU.bypass, ALU.mult,
                    accum_out=ew8[:, k:k + 1]))
            dots_v = eV.n
            p.wait("scalar", eV, dots_v)
            asil = p.S(lambda e: e.activation(sil8[:], ew8[:], AF.Silu))
            p.wait("vector", eS, asil)
            p.V(lambda e: e.scalar_tensor_tensor(w8[:], sil8[:], rsum[:], pexp[:], ALU.mult, ALU.mult))
            w8_v = eV.n
            p.wait("gpsimd", eV, dots_v)
            gn2 = None
            for k in range(8):
                gn2 = p.idma(ga[:, k * DM:(k + 1) * DM], ue_d[:, :],
                             idx32[:, k:k + 1], 0, gdma[i])
            p.wait("vector", gdma[i], gn2)
            if i >= 2:
                p.wait("vector", *store_done[i - 2])
            p.V(lambda e, par=par: e.tensor_scalar(
                yt[par][:], ga[:, 0:DM], w8[:, 0:1], None, ALU.mult))
            for k in range(1, 8):
                p.V(lambda e, k=k, par=par: e.scalar_tensor_tensor(
                    yt[par][:], ga[:, k * DM:(k + 1) * DM], w8[:, k:k + 1], yt[par][:], ALU.mult, ALU.add))
            yv = eV.n
            p.wait("sync", eV, yv)
            sts = str_[i]
            p.dma("sync", y_d[rows, :], yt[par][:], sts)
            store_done.append((sts, sts.n))
        for s in str_.sems:
            p.wait("sync", s, s.n)
        p.finish()
    return nc


def build_kv():
    nc = bass.Bass("TRN2", target_bir_lowering=False)
    xnT_d = nc.dram_tensor("xn1T", [DM, TPC], F32, kind="ExternalInput").ap()
    kw_d = nc.dram_tensor("k_wT", [DM, 128], F32, kind="ExternalInput").ap()
    vw_d = nc.dram_tensor("v_wT", [DM, 128], F32, kind="ExternalInput").ap()
    kT_d = nc.dram_tensor("kT", [128, TPC], BF16, kind="ExternalOutput").ap()
    vt_d = nc.dram_tensor("v_tok", [TPC, 128], BF16, kind="ExternalOutput").ap()
    ctx = ExitStack()
    with ctx:
        p = Prog(nc, ctx)
        eT, eV = p.e["tensor"], p.e["vector"]
        xnb = p.sbuf("xnb", [128, NDC * TPC], BF16)
        kwb = p.sbuf("kwb", [128, NDC * 128], BF16)
        vwb = p.sbuf("vwb", [128, NDC * 128], BF16)
        ko = p.sbuf("ko", [128, TPC], BF16)
        vo = p.sbuf("vo", [128, (TPC // 128) * 128], BF16)
        ps = [p.psum(f"ps{i}", [128, 512]) for i in range(2)]
        ld = p.sem("ld")
        st = p.sem("st")
        p.dma("gpsimd", xnb[:].rearrange("p (c t) -> p c t", c=NDC), fm_dram_ap(xnT_d, NDC), ld)
        p.dma("gpsimd", kwb[:].rearrange("p (c t) -> p c t", c=NDC), fm_dram_ap(kw_d, NDC), ld)
        p.dma("gpsimd", vwb[:].rearrange("p (c t) -> p c t", c=NDC), fm_dram_ap(vw_d, NDC), ld)
        p.wait("tensor", ld, ld.n)
        for th in range(2):
            mmn = None
            for c in range(NDC):
                mmn = p.T(lambda e, th=th, c=c: e.matmul(
                    ps[th % 2][:, :], kwb[:, c * 128:(c + 1) * 128],
                    xnb[:, c * TPC + th * 512: c * TPC + th * 512 + 512],
                    start=(c == 0), stop=(c == NDC - 1)))
            p.wait("vector", eT, mmn)
            ev = p.V(lambda e, th=th: e.tensor_copy(ko[:, th * 512:(th + 1) * 512], ps[th % 2][:, :]))
            p.wait("tensor", eV, ev)
        for i in range(TPC // 128):
            mmn = None
            for c in range(NDC):
                mmn = p.T(lambda e, i=i, c=c: e.matmul(
                    ps[i % 2][:, 0:128], xnb[:, c * TPC + i * 128: c * TPC + (i + 1) * 128],
                    vwb[:, c * 128:(c + 1) * 128], start=(c == 0), stop=(c == NDC - 1)))
            p.wait("vector", eT, mmn)
            ev = p.V(lambda e, i=i: e.tensor_copy(vo[:, i * 128:(i + 1) * 128], ps[i % 2][:, 0:128]))
            p.wait("tensor", eV, ev)
        p.wait("sync", eV, eV.n)
        p.dma("sync", kT_d[:], ko[:], st)
        p.dma("sync", vt_d.rearrange("(m p) d -> p m d", p=128), vo[:].rearrange("p (m d) -> p m d", d=128), st)
        p.wait("sync", st, st.n)
        p.finish()
    return nc


def build_attn():
    SCALE = float(1.0 / np.sqrt(128.0))
    NH = 16
    NMT = SEQ // 128
    nc = bass.Bass("TRN2", target_bir_lowering=False)
    xnT_d = nc.dram_tensor("xn1T", [DM, TPC], F32, kind="ExternalInput").ap()
    k_d = nc.dram_tensor("K", [128, SEQ], BF16, kind="ExternalInput").ap()
    v_d = nc.dram_tensor("V_tok", [SEQ, 128], BF16, kind="ExternalInput").ap()
    qw_d = nc.dram_tensor("qw_r", [NH, NDC, 128, 128], F32, kind="ExternalInput").ap()
    ow_d = nc.dram_tensor("ow_r", [NDC, NH, 128, 128], F32, kind="ExternalInput").ap()
    at_d = nc.dram_tensor("attnT", [DM, TPC], F32, kind="ExternalOutput").ap()
    ctx = ExitStack()
    with ctx:
        p = Prog(nc, ctx)
        eT, eV, eS = p.e["tensor"], p.e["vector"], p.e["scalar"]
        xnb = p.sbuf("xnb", [128, NDC * TPC], BF16)
        ksb = p.sbuf("ksb", [128, SEQ], BF16)
        vsb = p.sbuf("vsb", [128, NMT * 128], BF16)
        qsl = [p.sbuf(f"qsl{i}", [128, NDC * 128], BF16) for i in range(2)]
        osl = [p.sbuf(f"osl{i}", [128, NH * 128], BF16) for i in range(2)]
        qbf = p.sbuf("qbf", [128, NH * TPC], BF16)
        aob = p.sbuf("aob", [128, NH * TPC], BF16)
        q2 = p.sbuf("q2", [128, TPC])
        k2 = p.sbuf("k2", [128, SEQ])
        kn2 = p.sbuf("kn2", [1, SEQ])
        kmax2 = p.sbuf("kmax2", [1, 1])
        qn2 = p.sbuf("qn2", [1, TPC])
        mhat = p.sbuf("mhat", [1, TPC])
        rrow = p.sbuf("rrow", [1, 512])
        rbc = p.sbuf("rbc", [128, 512])
        scl = p.sbuf("scl", [128, 1])
        ones1f = p.sbuf("ones1f", [1, 128])
        onescf = p.sbuf("onescf", [128, 1])
        onesbf = p.sbuf("onesbf", [128, 1], BF16)
        prs = [p.sbuf(f"prs{i}", [128, 512], BF16) for i in range(2)]
        ao_st = [p.sbuf(f"ao_st{i}", [128, 512]) for i in range(2)]
        pss = [p.psum(f"pss{i}", [128, 512]) for i in range(2)]
        pssum = p.psum("pssum", [1, 512])
        psav = p.psum("psav", [128, 512])
        psbc = p.psum("psbc", [128, 512])
        ld = p.sem("ld")
        ldc = p.sem("ldc")
        ldq = Ring(p, "ldq", 2)
        str_ = Ring(p, "st", 2)

        p.V(lambda e: e.memset(ones1f[:], 1.0))
        p.V(lambda e: e.memset(onescf[:], 1.0))
        p.V(lambda e: e.memset(onesbf[:], 1.0))
        p.V(lambda e: e.memset(scl[:], SCALE))
        consts_v = p.e["vector"].n
        p.dma("gpsimd", xnb[:].rearrange("p (c t) -> p c t", c=NDC), fm_dram_ap(xnT_d, NDC), ldc)
        p.dma("sync", ksb[:], k_d[:], ld)
        p.dma("sync", vsb[:].rearrange("p (m d) -> p m d", d=128), v_d.rearrange("(m p) d -> p m d", p=128), ld)
        ldn0 = ld.n
        p.wait("tensor", ld, ldn0)
        p.wait("tensor", ldc, ldc.n)
        p.wait("scalar", ld, ldn0)
        p.wait("scalar", ldc, ldc.n)
        p.wait("vector", ld, ldn0)

        for oc in range(NH):
            par = oc % 2
            lds = ldq[oc]
            if oc >= 2:
                p.wait("gpsimd", eT, qmm_done[oc - 2])
            ldn = p.dma("gpsimd", qsl[par][:].rearrange("p (c i) -> p c i", c=NDC),
                        qw_d[oc].rearrange("c p i -> p c i"), lds)
            p.wait("tensor", lds, ldn)
            for th in range(2):
                mmn = None
                for c in range(NDC):
                    mmn = p.T(lambda e, par=par, c=c, th=th: e.matmul(
                        pss[th % 2][:, :], qsl[par][:, c * 128:(c + 1) * 128],
                        xnb[:, c * TPC + th * 512: c * TPC + th * 512 + 512],
                        start=(c == 0), stop=(c == NDC - 1)))
                if oc == 0 and th == 0:
                    qmm_done = {}
                qmm_done[oc] = mmn
                p.wait("vector", eT, mmn)
                ev = p.V(lambda e, oc=oc, th=th: e.tensor_scalar(
                    qbf[:, oc * TPC + th * 512: oc * TPC + th * 512 + 512],
                    pss[th % 2][:, :], scl[:], None, ALU.mult))
                p.wait("tensor", eV, ev)
        a1 = p.S(lambda e: e.activation(k2[:], ksb[:], AF.Square))
        p.wait("tensor", eS, a1)
        for j in range(8):
            mmn = p.T(lambda e, j=j: e.matmul(
                pssum[:, :], onescf[:], k2[:, j * 512:(j + 1) * 512], start=True, stop=True))
            p.wait("vector", eT, mmn)
            ev = p.V(lambda e, j=j: e.tensor_copy(kn2[:, j * 512:(j + 1) * 512], pssum[:, :]))
            p.wait("tensor", eV, ev)
        p.V(lambda e: e.reduce_max(kmax2[:], kn2[:], axis=AX.X))
        km_v = p.e["vector"].n
        st_i = 0
        store_done = []
        q_ready = p.e["vector"].n
        p.wait("scalar", eV, q_ready)
        for h in range(NH):
            a2 = p.S(lambda e, h=h: e.activation(q2[:], qbf[:, h * TPC:(h + 1) * TPC], AF.Square))
            p.wait("tensor", eS, a2)
            for th in range(2):
                mmn = p.T(lambda e, h=h, th=th: e.matmul(
                    pssum[:, :], onescf[:], q2[:, th * 512:(th + 1) * 512], start=True, stop=True))
                p.wait("vector", eT, mmn)
                ev = p.V(lambda e, th=th: e.tensor_copy(qn2[:, th * 512:(th + 1) * 512], pssum[:, :]))
                p.wait("tensor", eV, ev)
            p.wait("scalar", eV, km_v)
            p.wait("scalar", eV, p.e["vector"].n)
            amh = p.S(lambda e: e.activation(mhat[:], qn2[:], AF.Sqrt, scale=kmax2[:]))
            p.wait("vector", eS, amh)
            nmh = p.V(lambda e: e.tensor_single_scalar(mhat[:], mhat[:], -1.0, ALU.mult))
            p.wait("tensor", eV, nmh)
            for th in range(2):
                tq = slice(th * 512, (th + 1) * 512)
                for mt in range(NMT):
                    par = mt % 2
                    p.T(lambda e, h=h, mt=mt, tq=tq, par=par: e.matmul(
                        pss[par][:, :], ksb[:, mt * 128:(mt + 1) * 128],
                        qbf[:, h * TPC + tq.start: h * TPC + tq.stop], start=True, stop=False))
                    mmb = p.T(lambda e, tq=tq, par=par: e.matmul(
                        pss[par][:, :], ones1f[:, 0:128], mhat[:, tq], start=False, stop=True))
                    p.wait("scalar", eT, mmb)
                    aex = p.S(lambda e, par=par: e.activation(prs[par][:], pss[par][:, :], AF.Exp))
                    p.wait("tensor", eS, aex)
                    p.T(lambda e, mt=mt, par=par: e.matmul(
                        pssum[:, :], onesbf[:], prs[par][:], start=(mt == 0), stop=(mt == NMT - 1)))
                    p.T(lambda e, mt=mt, par=par: e.matmul(
                        psav[:, :], vsb[:, mt * 128:(mt + 1) * 128], prs[par][:],
                        start=(mt == 0), stop=(mt == NMT - 1)))
                avm = p.e["tensor"].n
                p.wait("vector", eT, avm)
                p.V(lambda e: e.reciprocal(rrow[:], pssum[:, :]))
                rv = p.e["vector"].n
                p.wait("tensor", eV, rv)
                bcm = p.T(lambda e: e.matmul(psbc[:, :], ones1f[:, 0:128], rrow[:], start=True, stop=True))
                p.wait("vector", eT, bcm)
                p.V(lambda e: e.tensor_copy(rbc[:], psbc[:, :]))
                ev = p.V(lambda e, h=h, tq=tq: e.tensor_tensor(
                    aob[:, h * TPC + tq.start: h * TPC + tq.stop], psav[:, :], rbc[:], ALU.mult))
                p.wait("tensor", eV, ev)
        ldo = Ring(p, "ldo", 2)
        omm_done = {}
        for dc in range(NDC):
            opar = dc % 2
            if dc >= 2:
                p.wait("gpsimd", eT, omm_done[dc - 2])
            ldn = p.dma("gpsimd", osl[opar][:].rearrange("p (c i) -> p c i", c=NH),
                        ow_d[dc].rearrange("c p i -> p c i"), ldo[dc])
            p.wait("tensor", ldo[dc], ldn)
            for th in range(2):
                mmn = None
                for oc in range(NH):
                    mmn = p.T(lambda e, dc=dc, th=th, oc=oc, opar=opar: e.matmul(
                        pss[(dc * 2 + th) % 2][:, :], osl[opar][:, oc * 128:(oc + 1) * 128],
                        aob[:, oc * TPC + th * 512: oc * TPC + th * 512 + 512],
                        start=(oc == 0), stop=(oc == NH - 1)))
                omm_done[dc] = mmn
                p.wait("vector", eT, mmn)
                if st_i >= 2:
                    p.wait("vector", *store_done[st_i - 2])
                ev = p.V(lambda e, dc=dc, th=th: e.tensor_copy(ao_st[(dc * 2 + th) % 2][:], pss[(dc * 2 + th) % 2][:, :]))
                p.wait("tensor", eV, ev)
                p.wait("sync", eV, ev)
                sts = str_[st_i]
                p.dma("sync", at_d[dc * 128:(dc + 1) * 128, th * 512:(th + 1) * 512],
                      ao_st[(dc * 2 + th) % 2][:], sts)
                store_done.append((sts, sts.n))
                st_i += 1
        for s in str_.sems:
            p.wait("sync", s, s.n)
        p.finish()
    return nc


def build_moe_final(debug=False):
    BIG = 65536.0
    nc = bass.Bass("TRN2", target_bir_lowering=False)
    xnT_d = nc.dram_tensor("xn1T", [DM, TPC], F32, kind="ExternalInput").ap()
    rw_d = nc.dram_tensor("router_wT", [DM, 16], F32, kind="ExternalInput").ap()
    eb_d = nc.dram_tensor("e_bias", [1, 16], F32, kind="ExternalInput").ap()
    wup_d = nc.dram_tensor("w_up_r", [16, 8, NDC, 128, 128], F32, kind="ExternalInput").ap()
    wdn_d = nc.dram_tensor("w_down_r", [16, NDC, 8, 128, 128], F32, kind="ExternalInput").ap()
    shu_d = nc.dram_tensor("shup_r", [32, NDC, 128, 128], F32, kind="ExternalInput").ap()
    shd_d = nc.dram_tensor("shdown_r", [NDC, 32, 128, 128], F32, kind="ExternalInput").ap()
    x1_d = nc.dram_tensor("x1T", [DM, TPC], F32, kind="ExternalInput").ap()
    at_d = nc.dram_tensor("attnT", [DM, TPC], F32, kind="ExternalInput").ap()
    out_d = nc.dram_tensor("outT", [DM, TPC], F32, kind="ExternalOutput").ap()
    if debug:
        dbgl_d = nc.dram_tensor("dbg_logits", [16, TPC], F32, kind="ExternalOutput").ap()
        dbgc_d = nc.dram_tensor("dbg_comb", [16, TPC], F32, kind="ExternalOutput").ap()
        dbga_d = nc.dram_tensor("dbg_acc", [DM, TPC], F32, kind="ExternalOutput").ap()
    ntiles = TPC // 128
    ctx = ExitStack()
    with ctx:
        p = Prog(nc, ctx)
        eT, eV, eS, eG = p.e["tensor"], p.e["vector"], p.e["scalar"], p.e["gpsimd"]
        xnb = p.sbuf("xnb", [128, NDC * TPC], BF16)
        acc = p.sbuf("acc", [128, NDC * TPC])
        act_e = p.sbuf("act_e", [128, 8 * TPC], BF16)
        upsl = p.sbuf("upsl", [128, 4 * NDC * 128], BF16)
        dnsl = p.sbuf("dnsl", [128, 8 * 8 * 128], BF16)
        rw = p.sbuf("rw", [128, NDC * 16])
        logitsT = p.sbuf("logitsT", [16, TPC])
        xs = [p.sbuf(f"xs{i}", [128, 512]) for i in range(2)]
        ident = p.sbuf("ident", [128, 128])
        iota16 = p.sbuf("iota16", [128, 16])
        esel = p.sbuf("esel16", [16, DM])
        tsel = p.sbuf("tsel16", [16, DM])
        ebr = p.sbuf("ebr", [1, 16])
        ebias_bc = p.sbuf("ebias_bc", [128, 16])
        combT = p.sbuf("combT", [16, TPC])
        combbc = p.sbuf("combbc", [128, TPC])
        sct = p.sbuf("sct", [128, 16])
        sbt = p.sbuf("sbt", [128, 16])
        ptmp = p.sbuf("ptmp", [128, 6 * 4])
        gs = p.sbuf("gs", [128, 4])
        gs8 = p.sbuf("gs8", [128, 8])
        gsort = p.sbuf("gsort", [128, 8])
        gmask = p.sbuf("gmask", [128, 4])
        mskd = p.sbuf("mskd", [128, 16])
        msk2 = p.sbuf("msk2", [128, 16])
        eqt = p.sbuf("eqt", [128, 16])
        eqi1 = p.sbuf("eqi1", [128, 16])
        eqi2 = p.sbuf("eqi2", [128, 16])
        tt = p.sbuf("tt", [128, 16])
        junk16 = p.sbuf("junk16", [128, 16])
        m1 = p.sbuf("m1", [128, 1])
        i1 = p.sbuf("i1", [128, 1])
        i2 = p.sbuf("i2", [128, 1])
        w1c = p.sbuf("w1c", [128, 1])
        w2c = p.sbuf("w2c", [128, 1])
        den = p.sbuf("den", [128, 1])
        rden = p.sbuf("rden", [128, 1])
        w1n = p.sbuf("w1n", [128, 1])
        w2n = p.sbuf("w2n", [128, 1])
        comb_t = p.sbuf("comb_t", [128, 16])
        r_sb = p.sbuf("r_sb", [128, 512])
        sq_sb = p.sbuf("sq_sb", [128, 512])
        x1c = [p.sbuf(f"x1c{i}", [128, 512]) for i in range(2)]
        atc = [p.sbuf(f"atc{i}", [128, 512]) for i in range(2)]
        ost = [p.sbuf(f"ost{i}", [128, 512]) for i in range(2)]
        psr = p.psum("psr", [16, 512])
        pst = p.psum("pst", [128, 512])
        psh = [p.psum(f"psh{i}", [128, 512]) for i in range(2)]
        psd = [p.psum(f"psd{i}", [128, 512]) for i in range(2)]
        ldc = p.sem("ldc")
        ld0 = p.sem("ld0")
        ldx = Ring(p, "ldx", 2)
        ldw = Ring(p, "ldw", 2)
        ldf = Ring(p, "ldf", 2)
        str_ = Ring(p, "st", 2)

        p.G(lambda e: e.memset(ident[:], 0.0))
        p.G(lambda e: e.affine_select(out=ident[:], in_=ident[:], compare_op=ALU.not_equal,
                                      fill=1.0, base=0, pattern=[[-1, 128]], channel_multiplier=1))
        p.G(lambda e: e.iota(iota16[:], pattern=[[1, 16]], base=0, channel_multiplier=0,
                             allow_small_or_imprecise_dtypes=True))
        p.G(lambda e: e.iota(esel[:].rearrange("k (n q) -> k n q", q=128),
                             pattern=[[1, 16], [0, 128]], base=0, channel_multiplier=0,
                             allow_small_or_imprecise_dtypes=True))
        gsel = p.G(lambda e: e.iota(tsel[:], pattern=[[0, DM]], base=0, channel_multiplier=1,
                                    allow_small_or_imprecise_dtypes=True))
        p.wait("vector", eG, gsel)
        esel_v = p.V(lambda e: e.tensor_tensor(esel[:], esel[:], tsel[:], ALU.is_equal))
        p.V(lambda e: e.memset(acc[:], 0.0))
        p.V(lambda e: e.memset(gs8[:], -BIG))
        p.dma("gpsimd", xnb[:].rearrange("p (c t) -> p c t", c=NDC), fm_dram_ap(xnT_d, NDC), ldc)
        p.dma("sync", rw[:].rearrange("p (c t) -> p c t", c=NDC), fm_dram_ap(rw_d, NDC), ld0)
        p.dma("sync", ebr[:], eb_d[:], ld0)
        ld0n = ld0.n
        p.wait("tensor", ld0, ld0n)
        p.wait("tensor", eV, esel_v)
        ones1f = p.sbuf("ones1f", [1, 128])
        ov = p.V(lambda e: e.memset(ones1f[:], 1.0))
        p.wait("tensor", eV, ov)
        bm = p.T(lambda e: e.matmul(pst[:, 0:16], ones1f[:], ebr[:], start=True, stop=True))
        p.wait("vector", eT, bm)
        ebv = p.V(lambda e: e.tensor_copy(ebias_bc[:], pst[:, 0:16]))
        p.wait("tensor", eV, ebv)

        for th in range(2):
            mmn = None
            for c in range(NDC):
                lds = ldx[th * NDC + c]
                ldn = p.dma("sync", xs[c % 2][:],
                            xnT_d[c * 128:(c + 1) * 128, th * 512:(th + 1) * 512], lds)
                p.wait("tensor", lds, ldn)
                mmn = p.T(lambda e, c=c: e.matmul(
                    psr[:, :], rw[:, c * 16:(c + 1) * 16], xs[c % 2][:],
                    start=(c == 0), stop=(c == NDC - 1)))
                if c < NDC - 1:
                    p.wait("sync", eT, mmn)
            p.wait("vector", eT, mmn)
            lv = p.V(lambda e, th=th: e.tensor_copy(logitsT[:, th * 512:(th + 1) * 512], psr[:, :]))
            p.wait("tensor", eV, lv)
        log_v = p.e["vector"].n
        p.wait("tensor", eV, log_v)

        gv = [(0, 1), (0, 2), (0, 3), (1, 2), (1, 3), (2, 3)]
        for i in range(ntiles):
            p.wait("tensor", eV, p.e["vector"].n)
            tr = p.T(lambda e, i=i: e.transpose(pst[:, 0:16], logitsT[:, i * 128:(i + 1) * 128], ident[0:16, 0:16]))
            p.wait("scalar", eT, tr)
            sg = p.S(lambda e: e.activation(sct[:], pst[:, 0:16], AF.Sigmoid))
            p.wait("vector", eS, sg)
            p.wait("tensor", eS, sg)
            p.V(lambda e: e.tensor_tensor(sbt[:], sct[:], ebias_bc[:], ALU.add))
            gvv = sbt[:].rearrange("p (g j) -> p g j", j=4)
            for pi, (a, b) in enumerate(gv):
                p.V(lambda e, a=a, b=b, pi=pi: e.tensor_tensor(
                    ptmp[:, pi * 4:(pi + 1) * 4],
                    sbt[:].rearrange("p (g j) -> p g j", j=4)[:, :, a],
                    sbt[:].rearrange("p (g j) -> p g j", j=4)[:, :, b], ALU.add))
            p.V(lambda e: e.tensor_tensor(gs[:], ptmp[:, 0:4], ptmp[:, 4:8], ALU.max))
            for pi in range(2, 6):
                p.V(lambda e, pi=pi: e.tensor_tensor(gs[:], gs[:], ptmp[:, pi * 4:(pi + 1) * 4], ALU.max))
            p.V(lambda e: e.tensor_copy(gs8[:, 0:4], gs[:]))
            p.V(lambda e: e.max(out=gsort[:], in_=gs8[:]))
            p.V(lambda e: e.tensor_scalar(gmask[:], gs[:], gsort[:, 1:2], None, ALU.is_ge))
            p.V(lambda e: e.tensor_tensor(
                mskd[:].rearrange("p (g j) -> p g j", j=4),
                sbt[:].rearrange("p (g j) -> p g j", j=4),
                gmask[:].rearrange("p (g o) -> p g o", o=1).to_broadcast([128, 4, 4]),
                ALU.mult))
            p.V(lambda e: e.reduce_max(m1[:], mskd[:], axis=AX.X))
            p.V(lambda e: e.tensor_scalar(eqt[:], mskd[:], m1[:], None, ALU.is_equal))
            p.V(lambda e: e.tensor_tensor(tt[:], eqt[:], iota16[:], ALU.mult))
            p.V(lambda e: e.scalar_tensor_tensor(tt[:], eqt[:], -BIG, tt[:], ALU.mult, ALU.add))
            p.V(lambda e: e.tensor_single_scalar(tt[:], tt[:], BIG, ALU.add))
            p.V(lambda e: e.tensor_reduce(i1[:], tt[:], axis=AX.X, op=ALU.min))
            p.V(lambda e: e.tensor_scalar(eqi1[:], iota16[:], i1[:], None, ALU.is_equal))
            p.V(lambda e: e.scalar_tensor_tensor(junk16[:], eqi1[:], 0.0, sct[:], ALU.bypass, ALU.mult,
                                                 accum_out=w1c[:]))
            p.V(lambda e: e.scalar_tensor_tensor(msk2[:], eqi1[:], -BIG, mskd[:], ALU.mult, ALU.add))
            p.V(lambda e: e.reduce_max(m1[:], msk2[:], axis=AX.X))
            p.V(lambda e: e.tensor_scalar(eqt[:], msk2[:], m1[:], None, ALU.is_equal))
            p.V(lambda e: e.tensor_tensor(tt[:], eqt[:], iota16[:], ALU.mult))
            p.V(lambda e: e.scalar_tensor_tensor(tt[:], eqt[:], -BIG, tt[:], ALU.mult, ALU.add))
            p.V(lambda e: e.tensor_single_scalar(tt[:], tt[:], BIG, ALU.add))
            p.V(lambda e: e.tensor_reduce(i2[:], tt[:], axis=AX.X, op=ALU.min))
            p.V(lambda e: e.tensor_scalar(eqi2[:], iota16[:], i2[:], None, ALU.is_equal))
            p.V(lambda e: e.scalar_tensor_tensor(junk16[:], eqi2[:], 0.0, sct[:], ALU.bypass, ALU.mult,
                                                 accum_out=w2c[:]))
            p.V(lambda e: e.tensor_tensor(den[:], w1c[:], w2c[:], ALU.add))
            p.V(lambda e: e.tensor_single_scalar(den[:], den[:], 1e-20, ALU.add))
            p.V(lambda e: e.reciprocal(rden[:], den[:]))
            p.V(lambda e: e.tensor_tensor(w1n[:], w1c[:], rden[:], ALU.mult))
            p.V(lambda e: e.tensor_tensor(w2n[:], w2c[:], rden[:], ALU.mult))
            p.V(lambda e: e.tensor_scalar(comb_t[:], eqi1[:], w1n[:], None, ALU.mult))
            cv = p.V(lambda e: e.scalar_tensor_tensor(comb_t[:], eqi2[:], w2n[:], comb_t[:], ALU.mult, ALU.add))
            p.wait("tensor", eV, cv)
            tr2 = p.T(lambda e, i=i: e.transpose(pst[0:16, 0:128], comb_t[:], ident[:]))
            p.wait("vector", eT, tr2)
            p.V(lambda e, i=i: e.tensor_copy(combT[:, i * 128:(i + 1) * 128], pst[0:16, 0:128]))
        comb_v = p.e["vector"].n
        p.wait("tensor", eV, comb_v)
        p.wait("tensor", ldc, ldc.n)

        if debug:
            dbg_st = p.sem("dbg_st")
            p.wait("sync", eV, comb_v)
            p.dma("sync", dbgl_d[:], logitsT[:], dbg_st)
            p.dma("sync", dbgc_d[:], combT[:], dbg_st)
            p.wait("sync", dbg_st, dbg_st.n)
        wl = 0
        upsl_free = None
        dnsl_free = None
        for ex in range(16):
            for th in range(2):
                bmn = p.T(lambda e, ex=ex, th=th: e.matmul(
                    pst[:, 0:512], esel[:, ex * 128:(ex + 1) * 128],
                    combT[:, th * 512:(th + 1) * 512], start=True, stop=True))
                p.wait("vector", eT, bmn)
                cbv = p.V(lambda e, th=th: e.tensor_copy(combbc[:, th * 512:(th + 1) * 512], pst[:, 0:512]))
                p.wait("tensor", eV, cbv)
            for ih in range(2):
                lds = ldw[wl]
                if upsl_free is not None:
                    p.wait("gpsimd", eV, upsl_free)
                ldn = p.dma("gpsimd", upsl[:].rearrange("p (a c i) -> p a c i", a=4, c=NDC),
                            wup_d[ex, ih * 4:(ih + 1) * 4].rearrange("a c p i -> p a c i"), lds)
                wl += 1
                p.wait("tensor", lds, ldn)
                for ic4 in range(4):
                    ic = ih * 4 + ic4
                    for th in range(2):
                        par = (ic * 2 + th) % 2
                        mmn = None
                        for dc in range(NDC):
                            mmn = p.T(lambda e, ic4=ic4, dc=dc, th=th, par=par: e.matmul(
                                psh[par][:, :], upsl[:, (ic4 * NDC + dc) * 128:(ic4 * NDC + dc + 1) * 128],
                                xnb[:, dc * TPC + th * 512: dc * TPC + th * 512 + 512],
                                start=(dc == 0), stop=(dc == NDC - 1)))
                        p.wait("scalar", eT, mmn)
                        rl = p.S(lambda e, par=par: e.activation(r_sb[:], psh[par][:, :], AF.Relu))
                        sq = p.S(lambda e: e.activation(sq_sb[:], r_sb[:], AF.Square))
                        p.wait("vector", eS, sq)
                        av = p.V(lambda e, ic=ic, th=th: e.tensor_tensor(
                            act_e[:, ic * TPC + th * 512: ic * TPC + th * 512 + 512],
                            sq_sb[:], combbc[:, th * 512:(th + 1) * 512], ALU.mult))
                        p.wait("tensor", eV, av)
                upsl_free = p.e["vector"].n
            for dh in range(2):
                lds = ldw[wl]
                if dnsl_free is not None:
                    p.wait("gpsimd", eT, dnsl_free)
                ldn = p.dma("gpsimd", dnsl[:].rearrange("p (a c i) -> p a c i", a=8, c=8),
                            wdn_d[ex, dh * 8:(dh + 1) * 8].rearrange("a c p i -> p a c i"), lds)
                wl += 1
                p.wait("tensor", lds, ldn)
                for dc8 in range(8):
                    dc = dh * 8 + dc8
                    for th in range(2):
                        par = (dc * 2 + th) % 2
                        mmn = None
                        for ic in range(8):
                            mmn = p.T(lambda e, dc8=dc8, ic=ic, th=th, par=par: e.matmul(
                                psd[par][:, :], dnsl[:, (dc8 * 8 + ic) * 128:(dc8 * 8 + ic + 1) * 128],
                                act_e[:, ic * TPC + th * 512: ic * TPC + th * 512 + 512],
                                start=(ic == 0), stop=(ic == 7)))
                        p.wait("vector", eT, mmn)
                        avv = p.V(lambda e, dc=dc, th=th, par=par: e.tensor_tensor(
                            acc[:, dc * TPC + th * 512: dc * TPC + th * 512 + 512],
                            psd[par][:, :],
                            acc[:, dc * TPC + th * 512: dc * TPC + th * 512 + 512], ALU.add))
                        p.wait("tensor", eV, avv)
                dnsl_free = p.e["tensor"].n

        for tq in range(4):
            qsl_ = slice(tq * 256, (tq + 1) * 256)
            for ih in range(8):
                lds = ldw[wl]
                if upsl_free is not None:
                    p.wait("gpsimd", eV, upsl_free)
                ldn = p.dma("gpsimd", upsl[:].rearrange("p (a c i) -> p a c i", a=4, c=NDC),
                            shu_d[ih * 4:(ih + 1) * 4].rearrange("a c p i -> p a c i"), lds)
                wl += 1
                p.wait("tensor", lds, ldn)
                for ic4 in range(4):
                    ic = ih * 4 + ic4
                    par = ic % 2
                    mmn = None
                    for dc in range(NDC):
                        mmn = p.T(lambda e, ic4=ic4, dc=dc, par=par, qsl_=qsl_: e.matmul(
                            psh[par][:, 0:256], upsl[:, (ic4 * NDC + dc) * 128:(ic4 * NDC + dc + 1) * 128],
                            xnb[:, dc * TPC + qsl_.start: dc * TPC + qsl_.stop],
                            start=(dc == 0), stop=(dc == NDC - 1)))
                    p.wait("scalar", eT, mmn)
                    rl = p.S(lambda e, par=par: e.activation(r_sb[:, 0:256], psh[par][:, 0:256], AF.Relu))
                    sq = p.S(lambda e: e.activation(sq_sb[:, 0:256], r_sb[:, 0:256], AF.Square))
                    p.wait("vector", eS, sq)
                    av = p.V(lambda e, ic=ic: e.tensor_copy(
                        act_e[:, ic * 256:(ic + 1) * 256], sq_sb[:, 0:256]))
                    p.wait("tensor", eV, av)
                upsl_free = p.e["vector"].n
            for dc2 in range(8):
                lds = ldw[wl]
                if dnsl_free is not None:
                    p.wait("gpsimd", eT, dnsl_free)
                ldn = p.dma("gpsimd", dnsl[:].rearrange("p (a c i) -> p a c i", a=2, c=32),
                            shd_d[dc2 * 2:(dc2 + 1) * 2].rearrange("a c p i -> p a c i"), lds)
                wl += 1
                p.wait("tensor", lds, ldn)
                for j in range(2):
                    dc = dc2 * 2 + j
                    par = dc % 2
                    mmn = None
                    for ic in range(32):
                        mmn = p.T(lambda e, j=j, ic=ic, par=par: e.matmul(
                            psd[par][:, 0:256], dnsl[:, (j * 32 + ic) * 128:(j * 32 + ic + 1) * 128],
                            act_e[:, ic * 256:(ic + 1) * 256],
                            start=(ic == 0), stop=(ic == 31)))
                    p.wait("vector", eT, mmn)
                    avv = p.V(lambda e, dc=dc, tq=tq, par=par: e.tensor_tensor(
                        acc[:, dc * TPC + tq * 256: dc * TPC + (tq + 1) * 256],
                        psd[par][:, 0:256],
                        acc[:, dc * TPC + tq * 256: dc * TPC + (tq + 1) * 256], ALU.add))
                    p.wait("tensor", eV, avv)
                dnsl_free = p.e["tensor"].n

        if debug:
            dbg_st2 = p.sem("dbg_st2")
            p.wait("sync", eV, p.e["vector"].n)
            p.dma("sync", fm_dram_ap(dbga_d, NDC), acc[:].rearrange("p (c t) -> p c t", c=NDC), dbg_st2)
            p.wait("sync", dbg_st2, dbg_st2.n)
        store_done = []
        fi = 0
        for dc in range(NDC):
            for th in range(2):
                par = fi % 2
                lds = ldf[fi]
                csl = slice(dc * TPC + th * 512, dc * TPC + th * 512 + 512)
                dsl_ = (slice(dc * 128, (dc + 1) * 128), slice(th * 512, (th + 1) * 512))
                if fi >= 2:
                    p.wait("sync", *store_done[fi - 2])
                p.dma("sync", x1c[par][:], x1_d[dsl_[0], dsl_[1]], lds)
                ldn = p.dma("sync", atc[par][:], at_d[dsl_[0], dsl_[1]], lds)
                p.wait("vector", lds, ldn)
                p.V(lambda e, par=par, csl=csl: e.tensor_tensor(ost[par][:], acc[:, csl], x1c[par][:], ALU.add))
                fv = p.V(lambda e, par=par: e.tensor_tensor(ost[par][:], ost[par][:], atc[par][:], ALU.add))
                p.wait("sync", eV, fv)
                sts = str_[fi]
                p.dma("sync", out_d[dsl_[0], dsl_[1]], ost[par][:], sts)
                store_done.append((sts, sts.n))
                fi += 1
        for s in str_.sems:
            p.wait("sync", s, s.n)
        p.finish()
    return nc


def _runs(nc, maps):
    return run_bass_kernel_spmd(nc, maps, core_ids=list(range(NCORES))).results


def kernel(**inputs):
    C = np.ascontiguousarray
    f32 = np.float32
    x = C(np.asarray(inputs["L0_attn_x"], f32).reshape(NCORES * TPC, DM))
    TT = NCORES * TPC

    def shards_rows(a):
        return [C(a[c * TPC:(c + 1) * TPC]) for c in range(NCORES)]

    def shards_cols(a):
        return [C(a[:, c * TPC:(c + 1) * TPC]) for c in range(NCORES)]

    nc = build_ln(0, False)
    res = _runs(nc, [dict(x=s) for s in shards_rows(x)])
    xn = np.concatenate([r["xn"] for r in res], axis=0)
    xnT = C(xn.T)

    W_inT = C(np.asarray(inputs["L0_attn_W_in"], f32).T)
    W_dtT = C(np.asarray(inputs["L0_attn_W_dt"], f32).T)
    b_dt_r = C(np.asarray(inputs["L0_attn_b_dt"], f32).reshape(16, 128).T)
    nc = build_dbc_delta()
    res = _runs(nc, [dict(xnT=s, W_inT=W_inT, W_dtT=W_dtT, b_dt_r=b_dt_r) for s in shards_cols(xnT)])
    dbcT = np.concatenate([r["dbcT"] for r in res], axis=1)
    deltaT = np.concatenate([r["deltaT"] for r in res], axis=1)

    W_q = C(np.asarray(inputs["L0_ffn_W_q"], f32))
    nc = build_qproj()
    res = _runs(nc, [dict(xnT=s, W_q=W_q) for s in shards_cols(xnT)])
    qfull = np.concatenate([r["qT"] for r in res], axis=1).T

    A_log = np.asarray(inputs["L0_attn_A_log"], f32)
    Dvec = np.asarray(inputs["L0_attn_D"], f32)
    Bt = C(dbcT[64:80])
    Ct = C(dbcT[80:96])
    nc = build_scan()
    maps = []
    for c in range(NCORES):
        sl = slice(c * 256, (c + 1) * 256)
        maps.append(dict(
            deltaT_sl=C(deltaT[sl]), xnT_sl=C(xnT[sl]), Bt=Bt, Ct=Ct,
            A_log_r=C(A_log[sl].reshape(2, 128, 16).transpose(1, 0, 2).reshape(128, 32)),
            D_r=C(Dvec[sl].reshape(2, 128).T)))
    res = _runs(nc, maps)
    y_mamba = np.concatenate([r["yT_sl"] for r in res], axis=0).T

    gate_r = C(np.asarray(inputs["L0_ffn_gate_w"], f32).reshape(16, 128, 64, 128).transpose(2, 0, 1, 3))
    up_r = C(np.asarray(inputs["L0_ffn_up_w"], f32).reshape(16, 128, 64, 128).transpose(2, 0, 1, 3))
    down_r = C(np.asarray(inputs["L0_ffn_down_w"], f32).reshape(64, 128, 16, 128).transpose(2, 0, 1, 3))
    nc = build_mlp()
    res = _runs(nc, [dict(xnT=s, gate_r=gate_r, up_r=up_r, down_r=down_r) for s in shards_cols(xnT)])
    mlp_tok = np.concatenate([r["mlpT"] for r in res], axis=1).T

    Q0 = C(qfull[:TT // 2].reshape(TT, 64))
    Q1 = C(qfull[TT // 2:].reshape(TT, 64))
    keys_r = C(np.asarray(inputs["L0_ffn_keys"], f32).transpose(1, 0, 2).reshape(64, 128))
    de_w = C(np.asarray(inputs["L0_ffn_down_embed"], f32))
    ue_w = C(np.asarray(inputs["L0_ffn_up_embed"], f32))
    nc = build_peer()
    maps = []
    for c in range(NCORES):
        sl = slice(c * TPC, (c + 1) * TPC)
        maps.append(dict(xn_tok=C(xn[sl]), Q0T=C(Q0[sl].T), Q1T=C(Q1[sl].T),
                         keys_r=keys_r, down_embed=de_w, up_embed=ue_w))
    res = _runs(nc, maps)
    peer_tok = np.concatenate([r["peer_tok"] for r in res], axis=0)

    nc = build_ln(3, True)
    maps = []
    for c in range(NCORES):
        sl = slice(c * TPC, (c + 1) * TPC)
        maps.append(dict(x=C(x[sl]), add0=C(y_mamba[sl]), add1=C(mlp_tok[sl]), add2=C(peer_tok[sl])))
    res = _runs(nc, maps)
    xn1 = np.concatenate([r["xn"] for r in res], axis=0)
    x1 = np.concatenate([r["xsum"] for r in res], axis=0)
    xn1T = C(xn1.T)
    x1T = C(x1.T)

    k_wT = C(np.asarray(inputs["L1_attn_k_w"], f32).T)
    v_wT = C(np.asarray(inputs["L1_attn_v_w"], f32).T)
    nc = build_kv()
    res = _runs(nc, [dict(xn1T=s, k_wT=k_wT, v_wT=v_wT) for s in shards_cols(xn1T)])
    kT = np.concatenate([r["kT"] for r in res], axis=1)
    v_tok = np.concatenate([r["v_tok"] for r in res], axis=0)

    qw_r = C(np.asarray(inputs["L1_attn_q_w"], f32).T.reshape(16, 128, 16, 128).transpose(2, 0, 1, 3))
    ow_r = C(np.asarray(inputs["L1_attn_o_w"], f32).T.reshape(16, 128, 16, 128).transpose(2, 0, 1, 3))
    nc = build_attn()
    maps = []
    for c in range(NCORES):
        b = c // 4
        Kb = C(kT[:, b * SEQ:(b + 1) * SEQ])
        Vb = C(v_tok[b * SEQ:(b + 1) * SEQ])
        maps.append(dict(xn1T=C(xn1T[:, c * TPC:(c + 1) * TPC]), K=Kb, V_tok=Vb,
                         qw_r=qw_r, ow_r=ow_r))
    res = _runs(nc, maps)
    attnT = np.concatenate([r["attnT"] for r in res], axis=1)

    rwT = C(np.asarray(inputs["L1_ffn_router_weight"], f32).T)
    ebias = C(np.asarray(inputs["L1_ffn_e_bias"], f32)[None, :])
    w_up_r = C(np.asarray(inputs["L1_ffn_w_up"], f32).reshape(16, 16, 128, 8, 128).transpose(0, 3, 1, 2, 4))
    w_down_r = C(np.asarray(inputs["L1_ffn_w_down"], f32).reshape(16, 8, 128, 16, 128).transpose(0, 3, 1, 2, 4))
    shup_r = C(np.asarray(inputs["L1_ffn_shared_up"], f32).reshape(16, 128, 32, 128).transpose(2, 0, 1, 3))
    shdown_r = C(np.asarray(inputs["L1_ffn_shared_down"], f32).reshape(32, 128, 16, 128).transpose(2, 0, 1, 3))
    nc = build_moe_final()
    maps = []
    for c in range(NCORES):
        sl = slice(c * TPC, (c + 1) * TPC)
        maps.append(dict(xn1T=C(xn1T[:, sl]), router_wT=rwT, e_bias=ebias,
                         w_up_r=w_up_r, w_down_r=w_down_r, shup_r=shup_r, shdown_r=shdown_r,
                         x1T=C(x1T[:, sl]), attnT=C(attnT[:, sl])))
    res = _runs(nc, maps)
    outT = np.concatenate([r["outT"] for r in res], axis=1)
    return C(outT.T.reshape(2, SEQ, DM)).astype(np.float32)
